# version 24 (final)
# speedup vs baseline: 1.0485x; 1.0003x over previous
import math
import numpy as np
import concourse.bass as bass
import concourse.mybir as mybir
from concourse.bass_utils import run_bass_kernel_spmd

F32 = mybir.dt.float32
BF16 = mybir.dt.bfloat16
I32 = mybir.dt.int32
U8 = mybir.dt.uint8
AX = mybir.AxisListType.X
AF = mybir.ActivationFunctionType
ALU = mybir.AluOpType

N_CORE = 8
SEQ_PER_CORE = 2
S = 2048
D = 1024
NT = S // 128
NC = D // 128
NEG = -1e30

LAUNCH_PLAN = [["l0_attn", "l0_ffn", "l1_attn", "l1_ffn"]]

DEBUG_DUMP = []
DEBUG_HEADS = ()
DEBUG_UNIT = None
L1F_HALVES = 2
L1F_JCN = 12
L1F_DOWN = True
L1F_EVAC = "stt"
L1F_GATES = True
L1F_SILU = True
L1A_DEFER = True
L1A_SCALEPB = False
FLASH_NOMAX = True
L1A_EARLYDUMP = False
L1A_TRUNC = 0
L1A_PREPMODE = 0



def _rope_tables(theta, rope_rows, pair_base, d_pairs, scale_rows_one):
    C = np.zeros((128, S), np.float64)
    Sp = np.zeros((128, S), np.float64)
    t = np.arange(S, dtype=np.float64)
    for i in range(d_pairs):
        ang = t / theta ** (2.0 * i / (2 * d_pairs))
        c, s = np.cos(ang), np.sin(ang)
        C[pair_base + 2 * i] = c
        C[pair_base + 2 * i + 1] = c
        Sp[pair_base + 2 * i] = s
        Sp[pair_base + 2 * i + 1] = -s
    for r in scale_rows_one:
        C[r] = 1.0
        Sp[r] = 0.0
    return C.astype(np.float32), Sp.astype(np.float32)


def _perm_matrix(swap_lo, swap_hi, ident_ranges):
    P = np.zeros((128, 128), np.float32)
    for i in range(swap_lo, swap_hi):
        P[i ^ 1, i] = 1.0
    for (a, b) in ident_ranges:
        for i in range(a, b):
            P[i, i] = 1.0
    return P


def _img(w, out_last=True):
    k, m = w.shape
    assert k % 128 == 0
    return np.ascontiguousarray(
        w.reshape(k // 128, 128, m).transpose(1, 0, 2).reshape(128, -1))


def _head_img(wT, dh):
    k, m = wT.shape
    h = m // dh
    return np.ascontiguousarray(
        wT.reshape(k // 128, 128, h, dh).transpose(2, 1, 0, 3).reshape(h, 128, (k // 128) * dh))


def make_consts(inp):
    c = {}
    f32 = np.float32
    c["c_ident"] = np.eye(128, dtype=f32)
    mask = np.full((128, 128), NEG, f32)
    c["c_mask"] = np.triu(mask, 1)
    C, Sp = _rope_tables(10000.0, 64, 0, 32, [])
    Cp = np.concatenate([C[0:64], C[0:64]], axis=0)
    Spp = np.concatenate([Sp[0:64], Sp[0:64]], axis=0)
    c["c_tabA"] = np.concatenate([Cp, Spp], axis=1)
    c["c_permA"] = _perm_matrix(0, 128, [])
    C2, S2 = _rope_tables(100000.0, 32, 32, 16, range(0, 32))
    C2p = np.concatenate([C2[0:64], C2[0:64]], axis=0)
    S2p = np.concatenate([S2[0:64], S2[0:64]], axis=0)
    c["c_tabB"] = np.concatenate([C2p, S2p], axis=1)
    p64 = _perm_matrix(32, 64, [(0, 32)])[0:64, 0:64]
    p128 = np.zeros((128, 128), np.float32)
    p128[0:64, 0:64] = p64
    p128[64:128, 64:128] = p64
    c["c_permB"] = p128
    c["c_l0a_wq"] = _head_img(np.ascontiguousarray(inp["L0_attn_Wq"].T), 128)
    c["c_l0a_wk"] = _head_img(np.ascontiguousarray(inp["L0_attn_Wk"].T), 128)
    c["c_l0a_wv"] = _head_img(np.ascontiguousarray(inp["L0_attn_Wv"].T), 256)
    c["c_l0a_wo"] = _img(np.ascontiguousarray(inp["L0_attn_Wo"].T))
    for nm in ("bq", "bk"):
        c[f"c_l0a_{nm}"] = np.ascontiguousarray(
            inp[f"L0_attn_{nm}"].reshape(8, 128).T)
    c["c_l0a_bv"] = inp["L0_attn_bv"].reshape(1, D)
    c["c_l0a_bo"] = inp["L0_attn_bo"].reshape(1, D)
    c["c_l0f_wq"] = _head_img(inp["L0_ffn_Wq"], 128)
    wkva = np.zeros((1024, 256), np.float32)
    wkva[:, 0:160] = inp["L0_ffn_Wkva"]
    c["c_l0f_wkva"] = _img(wkva)
    c["c_l0f_wkvb"] = inp["L0_ffn_Wkvb"]
    wo = inp["L0_ffn_Wo"].reshape(16, 64, D)[:, :32, :].reshape(512, D)
    c["c_l0f_wo"] = _img(np.ascontiguousarray(wo))
    c["c_l0f_wg8"] = np.stack([_img(inp["L0_ffn_Wg"][e]) for e in range(8)])
    c["c_l0f_wu8"] = np.stack([_img(inp["L0_ffn_Wu"][e]) for e in range(8)])
    c["c_l0f_wd8"] = np.stack([_img(inp["L0_ffn_Wd"][e]) for e in range(8)])
    c["c_l0f_swg"] = _img(inp["L0_ffn_sWg"][0])
    c["c_l0f_swu"] = _img(inp["L0_ffn_sWu"][0])
    c["c_l0f_swd"] = _img(inp["L0_ffn_sWd"][0])
    c["c_l0f_wgate"] = _img(inp["L0_ffn_Wgate"])
    c["c_l0f_ebias"] = inp["L0_ffn_expert_bias"].reshape(1, 8)
    import ml_dtypes
    c["c_l0f_wln1"] = inp["L0_ffn_w_ln1"].reshape(1, D).astype(ml_dtypes.bfloat16)
    c["c_l0f_wln2"] = inp["L0_ffn_w_ln2"].reshape(1, D).astype(ml_dtypes.bfloat16)
    c["c_l1a_wdq"] = _img(np.ascontiguousarray(inp["L1_attn_W_dq"].T))
    c["c_l1a_wdkv"] = _img(np.ascontiguousarray(inp["L1_attn_W_dkv"].T))
    c["c_l1a_wuq"] = _img(inp["L1_attn_W_uq"])
    c["c_l1a_wuk"] = _img(inp["L1_attn_W_uk"])
    c["c_l1a_wuv"] = _img(inp["L1_attn_W_uv"])
    c["c_l1a_wot"] = _img(np.ascontiguousarray(inp["L1_attn_W_o"].T))
    c["c_l1a_wdqraw"] = np.ascontiguousarray(inp["L1_attn_W_dq"])
    wg = np.ascontiguousarray(inp["L1_ffn_w_gate"].T)
    wu = np.ascontiguousarray(inp["L1_ffn_w_up"].T)
    wd = np.ascontiguousarray(inp["L1_ffn_w_down"].T)
    jimg = lambda w: np.ascontiguousarray(
        w.reshape(8, 128, 24, 128).transpose(1, 2, 0, 3).reshape(128, 24 * 1024))
    c["c_l1f_wg"] = jimg(wg)
    c["c_l1f_wu"] = jimg(wu)
    c["c_l1f_wd"] = np.ascontiguousarray(wd.reshape(24, 128, 1024))
    return c



ENGS = ("tensor", "vector", "scalar", "gpsimd", "sync")


class Region:
    __slots__ = ("name", "writers", "readers")

    def __init__(self, name):
        self.name = name
        self.writers = []
        self.readers = []


DMA_SLOTS = 8


class Prog:
    def __init__(self, nc, force_edges=False):
        self.nc = nc
        self.items = {e: [] for e in ENGS}
        dma_keys = tuple(f"dma_{q}{i}" for q in "sg" for i in range(DMA_SLOTS))
        self.cnt = {k: 0 for k in ENGS + dma_keys}
        self.dma_idx = {"sync": 0, "gpsimd": 0}
        self.lastw = {e: {} for e in ENGS}
        self.regions = {}
        self.force_edges = force_edges

    def region(self, name):
        r = self.regions.get(name)
        if r is None:
            r = self.regions[name] = Region(name)
        return r

    def _waits(self, eng, deps):
        need = {}
        for (k, v) in deps:
            if k == eng and not self.force_edges:
                continue
            if k == eng == "tensor":
                continue
            if need.get(k, 0) < v:
                need[k] = v
        out = []
        for k, v in sorted(need.items()):
            if self.lastw[eng].get(k, 0) >= v:
                continue
            self.lastw[eng][k] = v
            out.append((k, v))
        return out

    def _collect(self, eng, reads, writes, deps):
        deps = list(deps)
        rs = [self.region(r) if isinstance(r, str) else r for r in reads]
        ws = [self.region(w) if isinstance(w, str) else w for w in writes]
        for r in rs:
            deps += r.writers
        for w in ws:
            deps += w.writers + w.readers
        return rs, ws, self._waits(eng, deps)

    def _book(self, rs, ws, tok):
        for r in rs:
            r.readers.append(tok)
        for w in ws:
            w.writers = [tok]
            w.readers = []

    def emit(self, eng, fn, reads=(), writes=(), deps=()):
        rs, ws, waits = self._collect(eng, reads, writes, deps)
        self.cnt[eng] += 1
        tok = (eng, self.cnt[eng])
        self.items[eng].append((waits, fn, eng, 1))
        self._book(rs, ws, tok)
        return tok

    def dma(self, queue, fn, reads=(), writes=(), deps=()):
        q = "s" if queue == "sync" else "g"
        idx = self.dma_idx[queue]
        self.dma_idx[queue] = idx + 1
        key = f"dma_{q}{idx % DMA_SLOTS}"
        deps = list(deps)
        if self.cnt[key] > 0:
            deps.append((key, self.cnt[key]))
        rs, ws, waits = self._collect(queue, reads, writes, deps)
        self.cnt[key] += 16
        tok = (key, self.cnt[key])
        self.items[queue].append((waits, fn, key, 16))
        self._book(rs, ws, tok)
        return tok

    def barrier(self):
        snap = [(k, self.cnt[k]) for k in self.cnt if self.cnt[k] > 0]
        for e in ENGS:
            waits = self._waits(e, snap)
            if waits:
                self.items[e].append((waits, None, e, 0))

    def finish(self):
        for e, q in (("sync", "s"), ("gpsimd", "g")):
            waits = tuple((f"dma_{q}{i}", self.cnt[f"dma_{q}{i}"])
                          for i in range(DMA_SLOTS) if self.cnt[f"dma_{q}{i}"] > 0)
            if waits:
                self.items[e].append((waits, None, e, 0))

    def build_block(self):
        nc = self.nc
        import contextlib
        waited = {e: set() for e in ENGS}
        for e in ENGS:
            for waits, fn, inckey, incval in self.items[e]:
                for (k, v) in waits:
                    if k in waited:
                        waited[k].add(v)
        rank = {}
        for e in ENGS:
            rank[e] = {v: i + 1 for i, v in enumerate(sorted(waited[e]))}
        es = contextlib.ExitStack()
        sems = {}
        for k in self.cnt:
            if self.cnt[k] > 0 or any(
                    any(w[0] == k for w in it[0]) for e in ENGS for it in self.items[e]):
                sems[k] = es.enter_context(nc.semaphore(f"sem_{k}"))
        with nc.Block() as block:
            def mk(ename):
                def play(eng):
                    icnt = 0
                    for waits, fn, inckey, incval in self.items[ename]:
                        for (k, v) in waits:
                            if k in rank:
                                if v not in rank[k]:
                                    continue
                                eng.wait_ge(sems[k], rank[k][v])
                            else:
                                eng.wait_ge(sems[k], v)
                        if fn is not None:
                            ins = fn(eng)
                            if incval:
                                if inckey in rank:
                                    icnt += incval
                                    if icnt in waited[inckey]:
                                        ins.then_inc(sems[inckey], 1)
                                else:
                                    ins.then_inc(sems[inckey], incval)
                return play
            for ename in ENGS:
                if self.items[ename]:
                    getattr(block, ename)(mk(ename))
        es.close()



class Pool:

    def __init__(self, handle, nbytes):
        self.h = handle
        self.nbytes = nbytes
        self.off = 0
        self.marks = []

    def push(self):
        self.marks.append(self.off)

    def pop(self):
        self.off = self.marks.pop()

    def carve(self, p, n, dt, name=None):
        size = {F32: 4, BF16: 2, I32: 4, U8: 1}[dt]
        nb = n * size
        off = (self.off + 63) & ~63
        assert off + nb <= self.nbytes, f"pool overflow {name}: {off + nb} > {self.nbytes}"
        self.off = off + nb
        ap = self.h[0:p, off:off + nb]
        if dt is not U8:
            ap = ap.bitcast(dt)
        return ap



class Ctx:
    pass


def t_slice(ap, c0, n):
    return ap[:, c0:c0 + n]


def emit_load_x(ctx, seq):
    P = ctx.prog
    src = ctx.d_x[seq].rearrange("(tc p) c -> p tc c", p=128)
    dst = ctx.X.rearrange("p (tc c) -> p tc c", tc=NT)
    for g in range(4):
        P.dma("sync", lambda e, g=g: e.dma_start(
            dst[:, g * 4:(g + 1) * 4, :], src[:, g * 4:(g + 1) * 4, :]),
              writes=[f"X{g}"])


def emit_store_x(ctx, seq):
    P = ctx.prog
    dst = ctx.d_out[seq].rearrange("(tc p) c -> p tc c", p=128)
    srcv = ctx.X.rearrange("p (tc c) -> p tc c", tc=NT)
    for g in range(4):
        P.dma("sync", lambda e, g=g: e.dma_start(
            dst[:, g * 4:(g + 1) * 4, :], srcv[:, g * 4:(g + 1) * 4, :]),
              reads=[f"X{g}"])


def x_tile_region(tc):
    return f"X{tc // 4}"


def emit_layernorm(ctx, src_ap, src_regions, eps=1e-5):
    P = ctx.prog
    st = ctx.stats
    sq = ctx.sq
    xn = ctx.xn
    for tc in range(NT):
        tile = src_ap[:, tc * D:(tc + 1) * D]
        rsrc = [src_regions(tc)]
        P.emit("vector", lambda e, t=tile: e.reduce_sum(st[:, 0:1], t, AX),
               reads=rsrc, writes=["st"])
        P.emit("scalar", lambda e, t=tile: e.square(sq[:, :], t),
               reads=rsrc, writes=["sq"])
        P.emit("vector", lambda e: e.reduce_sum(st[:, 1:2], sq[:, :], AX),
               reads=["sq"], writes=["st"])

        P.emit("vector", lambda e: e.tensor_scalar(
            st[:, 2:3], st[:, 0:1], 1.0 / D, None, ALU.mult),
               reads=["st"], writes=["st"])
        P.emit("vector", lambda e: e.tensor_tensor(
            st[:, 3:4], st[:, 2:3], st[:, 2:3], ALU.mult),
               reads=["st"], writes=["st"])
        P.emit("vector", lambda e: e.scalar_tensor_tensor(
            st[:, 3:4], st[:, 1:2], 1.0 / D, st[:, 3:4], ALU.mult, ALU.subtract),
               reads=["st"], writes=["st"])
        P.emit("vector", lambda e: e.tensor_scalar(
            st[:, 3:4], st[:, 3:4], float(eps), None, ALU.add),
               reads=["st"], writes=["st"])
        P.emit("scalar", lambda e: e.activation(st[:, 4:5], st[:, 3:4], AF.Sqrt),
               reads=["st"], writes=["st"])
        P.emit("vector", lambda e: e.reciprocal(st[:, 4:5], st[:, 4:5]),
               reads=["st"], writes=["st"])
        P.emit("vector", lambda e, t=tile: e.tensor_scalar(
            xn[:, :], t, st[:, 2:3], st[:, 4:5], ALU.subtract, ALU.mult),
               reads=rsrc + ["st"], writes=["xn"])
        ptb = ctx.ps_pt_bf if tc % 2 == 0 else ctx.ps_ctx_bf
        preg = "ps_ptA" if tc % 2 == 0 else "ps_ctx"

        def tr_fn(e, ptb=ptb):
            ins = None
            for c in range(NC):
                ins = e.transpose(ptb[:, c * 128:(c + 1) * 128],
                                  xn[:, c * 128:(c + 1) * 128], ctx.identity)
            return ins
        P.emit("tensor", tr_fn, reads=["xn", "ident"],
               writes=[preg] + (["ps_ptB"] if tc % 2 == 0 else []))
        xnT3 = ctx.xnT_h3
        P.emit("vector", lambda e, t=tc, ptb=ptb: e.tensor_copy(
            xnT3[:, :, t * 128:t * 128 + 128],
            ptb.rearrange("p (c t) -> p c t", c=NC)),
               reads=[preg] + (["ps_ptB"] if tc % 2 == 0 else []),
               writes=[f"xnT{tc // 4}"])


def xnT_regions():
    return ["xnT0", "xnT1", "xnT2", "xnT3"]


def flash_phase1(ctx, qt, members):
    P = ctx.prog
    kv = (qt + 1) * 128
    nch = (kv + 511) // 512
    state = []
    for m in members:
        fb = m["fcol"]
        freg = f"fst{fb}"
        banks = []
        if FLASH_NOMAX and nch >= 2:
            aln = 4 if nch >= 3 else 2
            if ctx.sbank_ctr % aln != 0:
                ctx.sbank_ctr += aln - (ctx.sbank_ctr % aln)
        for ci in range(nch):
            c0 = ci * 512
            w = min(512, kv - c0)
            b = ctx.sbank_ctr % 4
            ctx.sbank_ctr += 1
            bap = ctx.ps_s[:, b * 512: b * 512 + w]
            banks.append((b, bap, c0, w, ci))
            diag = ci == nch - 1

            def sc_fn(e, m=m, bap=bap, c0=c0, w=w, qt=qt, diag=diag, b=b):
                ins = e.matmul(bap, m["q_ap"], m["k_ap"][:, c0:c0 + w],
                               start=True, stop=not diag)
                if diag:
                    off = qt * 128 - c0
                    ins = e.matmul(ctx.ps_s[:, b * 512 + off: b * 512 + off + 128],
                                   ctx.identity, ctx.mask, start=False, stop=True,
                                   skip_group_check=True)
                return ins
            P.emit("tensor", sc_fn,
                   reads=m["q_reads"] + m["k_reads"] + ["ident", "mask"],
                   writes=[f"ps_c{b}"])
            if not FLASH_NOMAX:
                P.emit("vector", lambda e, bap=bap, fb=fb, ci=ci, neg=(nch == 1):
                       e.reduce_max(ctx.fst[:, fb + ci: fb + ci + 1], bap, AX,
                                    negate=neg),
                       reads=[f"ps_c{b}"], writes=[freg])
        if FLASH_NOMAX:
            pass
        elif nch == 1:
            P.emit("vector", lambda e, fb=fb: e.tensor_scalar(
                ctx.fst[:, fb + 4: fb + 5], ctx.fst[:, fb: fb + 1], 0.125,
                None, ALU.mult),
                   reads=[freg], writes=[freg])
        else:
            P.emit("vector", lambda e, fb=fb, nch=nch: e.reduce_max(
                ctx.fst[:, fb + 4: fb + 5], ctx.fst[:, fb: fb + nch], AX),
                   reads=[freg], writes=[freg])
            P.emit("vector", lambda e, fb=fb: e.tensor_scalar(
                ctx.fst[:, fb + 4: fb + 5], ctx.fst[:, fb + 4: fb + 5], -0.125,
                None, ALU.mult),
                   reads=[freg], writes=[freg])
        state.append(banks)
    return state


def flash_phase2(ctx, qt, members, state, eps, scale_pb=True):
    P = ctx.prog
    kv = (qt + 1) * 128
    for m, banks in zip(members, state):
        fb = m["fcol"]
        freg = f"fst{fb}"
        nch = len(banks)
        b0 = banks[0][0] if banks else 0
        merged = (FLASH_NOMAX and nch >= 2 and
                  b0 + nch <= 4 and all(banks[i][0] == b0 + i for i in range(nch)))
        if merged:
            bregs = [f"ps_c{b0 + i}" for i in range(nch)]
            P.emit("scalar", lambda e, m=m, fb=fb, kv=kv, b0=b0:
                   e.activation(m["pb"][:, 0:kv],
                                ctx.ps_s[:, b0 * 512: b0 * 512 + kv], AF.Exp,
                                scale=ctx.scl[:, 0:1],
                                accum_out=ctx.fst[:, fb + 5: fb + 6]),
                   reads=bregs + ["scl"],
                   writes=[m["pbreg"], freg])
            nch = 1
            banks = []
        for (b, bap, c0, w, ci) in banks:
            if FLASH_NOMAX:
                P.emit("scalar", lambda e, m=m, bap=bap, c0=c0, w=w, fb=fb, ci=ci:
                       e.activation(m["pb"][:, c0:c0 + w], bap, AF.Exp,
                                    scale=ctx.scl[:, 0:1],
                                    accum_out=ctx.fst[:, fb + 5 + ci: fb + 6 + ci]),
                       reads=[f"ps_c{b}", "scl"], writes=[m["pbreg"], freg])
            else:
                P.emit("scalar", lambda e, m=m, bap=bap, c0=c0, w=w, fb=fb, ci=ci:
                       e.activation(m["pb"][:, c0:c0 + w], bap, AF.Exp,
                                    bias=ctx.fst[:, fb + 4: fb + 5],
                                    scale=ctx.scl[:, 0:1],
                                    accum_out=ctx.fst[:, fb + 5 + ci: fb + 6 + ci]),
                       reads=[f"ps_c{b}", freg, "scl"], writes=[m["pbreg"], freg])
        sumc = fb + 5 if nch == 1 else fb + 9
        if nch > 1:
            P.emit("vector", lambda e, fb=fb, nch=nch: e.reduce_sum(
                ctx.fst[:, fb + 9: fb + 10], ctx.fst[:, fb + 5: fb + 5 + nch], AX),
                   reads=[freg], writes=[freg])
        if eps:
            P.emit("vector", lambda e, fb=fb, sumc=sumc: e.tensor_scalar(
                ctx.fst[:, sumc: sumc + 1], ctx.fst[:, sumc: sumc + 1],
                float(eps), None, ALU.add),
                   reads=[freg], writes=[freg])
        P.emit("vector", lambda e, fb=fb, sumc=sumc: e.reciprocal(
            ctx.fst[:, fb + 10: fb + 11], ctx.fst[:, sumc: sumc + 1]),
               reads=[freg], writes=[freg])
        if scale_pb:
            P.emit("vector", lambda e, m=m, fb=fb, kv=kv: e.tensor_scalar(
                m["pb"][:, 0:kv], m["pb"][:, 0:kv], ctx.fst[:, fb + 10: fb + 11],
                None, ALU.mult),
                   reads=[freg, m["pbreg"]], writes=[m["pbreg"]])


def flash_trailing(ctx, qt, members, av_lhsT, av_reads, evac_fn):
    P = ctx.prog
    G = len(members)
    nkc = qt + 1
    kgrp = 4 // G
    PT2 = ctx.PT2
    prev_av = None
    zones = [(ctx.ps_pt_bf, 0, "ps_ptA"), (ctx.ps_y_bf, 1024, "ps_y1")]
    for g0 in range(0, nkc, kgrp):
        gn = min(kgrp, nkc - g0)
        zbuf, hoff, hreg = zones[ctx.pt_half]
        ctx.pt_half ^= 1

        def tr_fn(e, g0=g0, gn=gn, hoff=hoff, zbuf=zbuf):
            ins = None
            for i in range(gn):
                for mi, m in enumerate(members):
                    c = hoff + (i * G + mi) * 128
                    ins = e.transpose(
                        zbuf[:, c: c + 128],
                        m["pb"][:, (g0 + i) * 128:(g0 + i + 1) * 128],
                        ctx.identity)
            return ins
        P.emit("tensor", tr_fn,
               reads=[m["pbreg"] for m in members] + ["ident"],
               writes=[hreg])
        par = ctx.pt_par
        ctx.pt_par ^= 1
        pt = PT2[par]
        ctx.evac_ctr += 1
        if ctx.evac_ctr % 2 == 0:
            P.emit("vector", lambda e, pt=pt, gn=gn, hoff=hoff, zbuf=zbuf:
                   e.tensor_copy(pt[:, 0:gn * G * 128],
                                 zbuf[:, hoff: hoff + gn * G * 128]),
                   reads=[hreg], writes=[f"PT{par}"])
        else:
            P.emit("scalar", lambda e, pt=pt, gn=gn, hoff=hoff, zbuf=zbuf:
                   e.copy(pt[:, 0:gn * G * 128],
                          zbuf[:, hoff: hoff + gn * G * 128]),
                   reads=[hreg], writes=[f"PT{par}"])
        if prev_av is not None:
            P.emit("tensor", prev_av[0], reads=prev_av[1], writes=["ps_ctx"])

        def av_fn(e, g0=g0, gn=gn, par=par, nkc=nkc):
            ins = None
            for i in range(gn):
                kc = g0 + i
                ins = e.matmul(ctx.ps_ctx[:, 0:G * 128], av_lhsT(kc),
                               PT2[par][:, i * G * 128:(i + 1) * G * 128],
                               start=(kc == 0), stop=(kc == nkc - 1))
            return ins
        prev_av = (av_fn, [f"PT{par}"] + av_reads)
    P.emit("tensor", prev_av[0], reads=prev_av[1], writes=["ps_ctx"])
    evac_fn()


def emit_l0_attn(ctx, seq):
    P = ctx.prog
    pool = ctx.pool
    P.barrier()
    pool.push()
    ctxb = pool.carve(128, NC * S, BF16, "ctxb")
    tab = pool.carve(128, 2 * S, BF16, "tab")
    emit_layernorm(ctx, ctx.X, x_tile_region)

    pool.push()
    Qp = [pool.carve(128, S, BF16, f"Qp{i}") for i in range(2)]
    KA = pool.carve(128, S, BF16, "KA0")
    KB = pool.carve(128, S, BF16, "KB0")
    Vq = pool.carve(128, NT * 256, BF16, "Vq")
    PbA = pool.carve(128, S, BF16, "PbA0")
    PbB = pool.carve(128, S, BF16, "PbB0")
    PbA2 = ctx.sq[:, :].bitcast(BF16)
    PbB2 = pool.carve(128, S, BF16, "PbB1")
    ctx.PT2 = [pool.carve(128, 512, BF16, f"PT2_{i}") for i in range(2)]
    wq = [pool.carve(128, 1024, BF16, f"wq{i}") for i in range(2)]
    wk = [pool.carve(128, 1024, BF16, f"wk{i}") for i in range(2)]
    wv = pool.carve(128, 2048, BF16, "wv")
    bq2 = pool.carve(128, 8, F32, "bq2")
    bk2 = pool.carve(128, 8, F32, "bk2")
    bvq = pool.carve(128, 256, F32, "bvq")

    P.dma("gpsimd", lambda e: e.dma_start(tab[:, :], ctx.d_c["c_tabA"][:, :]),
          writes=["tab"])
    P.dma("gpsimd", lambda e: e.dma_start(ctx.perm[:, :], ctx.d_c["c_permA"][:, :]),
          writes=["perm"])
    P.dma("sync", lambda e: e.dma_start(bq2[:, :], ctx.d_c["c_l0a_bq"][:, :]),
          writes=["bq2"])
    P.dma("sync", lambda e: e.dma_start(bk2[:, :], ctx.d_c["c_l0a_bk"][:, :]),
          writes=["bk2"])

    P.emit("vector", lambda e: e.memset(KA[64:128, :], 0.0), writes=["KA0"])
    P.emit("vector", lambda e: e.memset(KB[0:64, :], 0.0), writes=["KB0"])

    def proj_pair_qk(p, par, qouts, kouts):
        P.dma("gpsimd", lambda e, p=p: e.dma_start(wq[par][:, :],
                                                   ctx.d_c["c_l0a_wq"][p]),
              writes=[f"wq{par}"])
        P.dma("gpsimd", lambda e, p=p: e.dma_start(wk[par][:, :],
                                                   ctx.d_c["c_l0a_wk"][p]),
              writes=[f"wk{par}"])
        tasks = []
        for g in range(4):
            tasks.append(("Q", g))
            tasks.append(("K", g))

        def emit_tail(prev):
            which, g, ys, tp = prev
            gs = slice(g * 512, (g + 1) * 512)
            P.emit("tensor", lambda e, tp=tp: e.matmul(
                ctx.ps_ctx[:, 0:512], ctx.perm[:, :], ctx.T2[tp][:, :],
                start=True, stop=True),
                   reads=[f"t2_{tp}", "perm"], writes=["ps_ctx"])
            outs = qouts if which == "Q" else kouts
            for dst_fn, dst_reg, (r0, r1) in outs:
                P.emit("vector", lambda e, gs=gs, dst_fn=dst_fn, r0=r0, r1=r1,
                       tp=tp: e.tensor_tensor(
                           dst_fn(gs)[r0:r1, :], ctx.T1[tp][r0:r1, :],
                           ctx.ps_ctx[r0:r1, 0:512], ALU.add),
                       reads=[f"t1_{tp}", "ps_ctx"], writes=[dst_reg])

        prev = None
        for i, (which, g) in enumerate(tasks):
            wbuf = wq[par] if which == "Q" else wk[par]
            wreg = (f"wq{par}" if which == "Q" else f"wk{par}")
            bias_ap = bq2[:, p:p + 1] if which == "Q" else bk2[:, p:p + 1]
            bias_reg = "bq2" if which == "Q" else "bk2"
            ys = i % 2
            tp = i % 2
            ps = ctx.ps_y[:, ys * 512:(ys + 1) * 512]

            def mm_fn(e, g=g, ps=ps, wbuf=wbuf):
                ins = None
                for c in range(NC):
                    ins = e.matmul(ps, wbuf[:, c * 128:(c + 1) * 128],
                                   ctx.xnT[:, c * S + g * 512: c * S + (g + 1) * 512],
                                   start=(c == 0), stop=(c == NC - 1))
                return ins
            P.emit("tensor", mm_fn, reads=[wreg, f"xnT{g}"],
                   writes=[f"ps_y{ys}"])
            if prev is not None:
                emit_tail(prev)
            gs = slice(g * 512, (g + 1) * 512)

            def rope1_fn(e, gs=gs, ps=ps, bias_ap=bias_ap, tp=tp):
                e.scalar_tensor_tensor(ctx.T1[tp][:, :], ps, bias_ap,
                                       tab[:, gs], ALU.add, ALU.mult)
                gs2 = slice(S + gs.start, S + gs.stop)
                return e.scalar_tensor_tensor(ctx.T2[tp][:, :], ps, bias_ap,
                                              tab[:, gs2], ALU.add, ALU.mult)
            P.emit("vector", rope1_fn, reads=[f"ps_y{ys}", "tab", bias_reg],
                   writes=[f"t1_{tp}", f"t2_{tp}"])
            prev = (which, g, ys, tp)
        emit_tail(prev)

    def build_vquad(q):
        P.dma("gpsimd", lambda e, q=q: e.dma_start(wv[:, :], ctx.d_c["c_l0a_wv"][q]),
              writes=["wv"])
        P.dma("sync", lambda e, q=q: e.dma_start(
            bvq[:, :],
            ctx.d_c["c_l0a_bv"][0:1, q * 256:(q + 1) * 256].to_broadcast((128, 256))),
              writes=["bvq"])
        for tc in range(NT):
            ys = tc % 2
            ps = ctx.ps_y[:, ys * 512: ys * 512 + 256]

            def mm_fn(e, tc=tc, ps=ps):
                ins = None
                for c in range(NC):
                    ins = e.matmul(ps, ctx.xnT[:, c * S + tc * 128: c * S + tc * 128 + 128],
                                   wv[:, c * 256:(c + 1) * 256],
                                   start=(c == 0), stop=(c == NC - 1))
                return ins
            P.emit("tensor", mm_fn, reads=["wv", f"xnT{tc // 4}"], writes=[f"ps_y{ys}"])
            P.emit("vector", lambda e, tc=tc, ps=ps: e.tensor_tensor(
                Vq[:, tc * 256:(tc + 1) * 256], ps, bvq[:, :], ALU.add),
                   reads=[f"ps_y{ys}", "bvq"], writes=["Vq"])

    pendq = []
    ui = 0
    for p in range(8):
        par = p % 2
        if p % 2 == 0:
            for pp in pendq:
                flash_trailing(ctx, *pp)
            pendq = []
            build_vquad(p // 2)
        proj_pair_qk(
            p, par,
            [(lambda gs, par=par: Qp[par][:, gs], f"Qp{par}", (0, 128))],
            [(lambda gs: KA[:, gs], "KA0", (0, 64)),
             (lambda gs: KB[:, gs], "KB0", (64, 128))])
        for qt in range(NT):
            upar = ui % 2
            pbA = PbA if upar == 0 else PbA2
            pbB = PbB if upar == 0 else PbB2
            members = [
                {"q_ap": Qp[par][:, qt * 128: qt * 128 + 128],
                 "q_reads": [f"Qp{par}"], "k_ap": KA,
                 "k_reads": ["KA0"], "pb": pbA, "pbreg": f"PbA{upar}",
                 "fcol": upar * 64},
                {"q_ap": Qp[par][:, qt * 128: qt * 128 + 128],
                 "q_reads": [f"Qp{par}"], "k_ap": KB,
                 "k_reads": ["KB0"], "pb": pbB, "pbreg": f"PbB{upar}",
                 "fcol": upar * 64 + 16}]
            pairh = p % 2

            def evac(p=p, qt=qt):
                P.emit("scalar", lambda e: e.copy(
                    ctxb[0:64, p * S + qt * 128: p * S + qt * 128 + 128],
                    ctx.ps_ctx[0:64, 0:128]),
                       reads=["ps_ctx"], writes=[f"ctxb{p}"])
                P.emit("scalar", lambda e: e.copy(
                    ctxb[64:128, p * S + qt * 128: p * S + qt * 128 + 128],
                    ctx.ps_ctx[64:128, 128:256]),
                       reads=["ps_ctx"], writes=[f"ctxb{p}"])
            av = lambda kc, pairh=pairh: Vq[:, kc * 256 + pairh * 128:
                                            kc * 256 + pairh * 128 + 128]
            stA = flash_phase1(ctx, qt, members[0:1])
            if len(pendq) >= 2:
                flash_trailing(ctx, *pendq.pop(0))
            flash_phase2(ctx, qt, members[0:1], stA, eps=1e-8)
            stB = flash_phase1(ctx, qt, members[1:2])
            flash_phase2(ctx, qt, members[1:2], stB, eps=1e-8)
            pendq.append((qt, members, av, ["Vq"], evac))
            ui += 1
    for pp in pendq:
        flash_trailing(ctx, *pp)
    pendq = []

    P.barrier()
    pool.pop()
    pool.push()
    woT = pool.carve(128, 8 * D, BF16, "woT")
    bo_bc = pool.carve(128, D, F32, "bo_bc")
    for c in range(NC):
        P.dma("gpsimd", lambda e, c=c: e.dma_start(
            woT[:, c * D:(c + 1) * D],
            ctx.d_c["c_l0a_wo"][:, c * D:(c + 1) * D]), writes=["woT"])
    P.dma("sync", lambda e: e.dma_start(
        bo_bc[:, :], ctx.d_c["c_l0a_bo"][:, :].to_broadcast((128, D))),
          writes=["bo_bc"])
    for tc in range(NT):
        P.emit("vector", lambda e, tc=tc: e.tensor_tensor(
            ctx.X[:, tc * D:(tc + 1) * D], ctx.X[:, tc * D:(tc + 1) * D],
            bo_bc[:, :], ALU.add),
               reads=["bo_bc", x_tile_region(tc)], writes=[x_tile_region(tc)])
    for qt in range(NT):
        for half in range(2):
            ps = ctx.ps_y[:, half * 512:(half + 1) * 512]

            def wo_fn(e, qt=qt, half=half, ps=ps):
                ins = None
                for c in range(NC):
                    ins = e.matmul(ps, ctxb[:, c * S + qt * 128: c * S + qt * 128 + 128],
                                   woT[:, c * D + half * 512: c * D + (half + 1) * 512],
                                   start=(c == 0), stop=(c == NC - 1))
                return ins
            P.emit("tensor", wo_fn,
                   reads=["woT", f"ctxb{0}", f"ctxb{1}", f"ctxb{2}", f"ctxb{3}",
                          f"ctxb{4}", f"ctxb{5}", f"ctxb{6}", f"ctxb{7}"],
                   writes=[f"ps_y{half}"])
            P.emit("vector", lambda e, qt=qt, half=half, ps=ps: e.tensor_tensor(
                ctx.X[:, qt * D + half * 512: qt * D + (half + 1) * 512],
                ctx.X[:, qt * D + half * 512: qt * D + (half + 1) * 512],
                ps, ALU.add),
                   reads=[f"ps_y{half}", x_tile_region(qt)],
                   writes=[x_tile_region(qt)])
    P.barrier()
    pool.pop()
    pool.pop()




def emit_softmax_unit(ctx, qt, kv, eps):
    P = ctx.prog
    st = ctx.stats2
    smax = ctx.ps_s
    P.emit("vector", lambda e, kv=kv: e.reduce_max(
        st[:, 0:1], smax[:, 0:kv], AX, negate=True),
           reads=["ps_s"], writes=["st2"])
    P.emit("vector", lambda e: e.tensor_scalar(
        st[:, 1:2], st[:, 0:1], 0.125, None, ALU.mult),
           reads=["st2"], writes=["st2"])
    P.emit("scalar", lambda e, kv=kv: e.activation(
        ctx.Pb[:, 0:kv], smax[:, 0:kv], AF.Exp, bias=st[:, 1:2],
        scale=ctx.scl[:, 0:1], accum_out=st[:, 2:3]),
           reads=["ps_s", "st2", "scl"], writes=["Pb", "st2"])
    if eps:
        P.emit("vector", lambda e: e.tensor_scalar(
            st[:, 3:4], st[:, 2:3], float(eps), None, ALU.add),
               reads=["st2"], writes=["st2"])
        P.emit("vector", lambda e: e.reciprocal(st[:, 3:4], st[:, 3:4]),
               reads=["st2"], writes=["st2"])
    else:
        P.emit("vector", lambda e: e.reciprocal(st[:, 3:4], st[:, 2:3]),
               reads=["st2"], writes=["st2"])
    P.emit("vector", lambda e, kv=kv: e.tensor_scalar(
        ctx.Psb[:, 0:kv], ctx.Pb[:, 0:kv], st[:, 3:4], None, ALU.mult),
           reads=["Pb", "st2"], writes=["Psb"])


def emit_scores(ctx, qh_ap, qh_reg, k_ap, k_reg, qt, kv, q_col=None):
    P = ctx.prog
    smax = ctx.ps_s
    npc = (kv + 511) // 512
    if q_col is None:
        q_col = qt * 128

    def scores_fn(e, qt=qt, kv=kv, npc=npc, q_col=q_col):
        ins = None
        for pc in range(npc):
            w = min(512, kv - pc * 512)
            last = pc == npc - 1
            ins = e.matmul(smax[:, pc * 512:pc * 512 + w],
                           qh_ap[:, q_col:q_col + 128],
                           k_ap[:, pc * 512:pc * 512 + w],
                           start=True, stop=not last)
            if last:
                ins = e.matmul(smax[:, qt * 128:qt * 128 + 128],
                               ctx.identity, ctx.mask,
                               start=False, stop=True, skip_group_check=True)
        return ins
    P.emit("tensor", scores_fn, reads=[qh_reg, k_reg, "ident", "mask"],
           writes=["ps_s"])


def emit_pt_av(ctx, qt, av_lhsT_fn, lhsT_reads, ctx_ps):
    P = ctx.prog
    nkc = qt + 1
    for kc0 in range(0, nkc, 8):
        kn = min(8, nkc - kc0)

        def tr_fn(e, kc0=kc0, kn=kn):
            ins = None
            for i in range(kn):
                ins = e.transpose(
                    ctx.ps_pt_bf[:, i * 128:(i + 1) * 128],
                    ctx.Psb[:, (kc0 + i) * 128:(kc0 + i + 1) * 128],
                    ctx.identity)
            return ins
        P.emit("tensor", tr_fn, reads=["Psb", "ident"],
               writes=["ps_ptA", "ps_ptB"])
        P.emit("scalar", lambda e, kn=kn: e.copy(
            ctx.PT[:, 0:kn * 128], ctx.ps_pt_bf[:, 0:kn * 128]),
               reads=["ps_ptA", "ps_ptB"], writes=["PT"])

        def av_fn(e, kc0=kc0, kn=kn, nkc=nkc):
            ins = None
            for i in range(kn):
                kc = kc0 + i
                ins = e.matmul(ctx_ps, av_lhsT_fn(kc),
                               ctx.PT[:, i * 128:(i + 1) * 128],
                               start=(kc == 0), stop=(kc == nkc - 1))
            return ins
        P.emit("tensor", av_fn, reads=["PT"] + lhsT_reads, writes=["ps_ctx"])


def emit_rms_to_xnT(ctx, wbc_ap, wbc_reg, eps=1e-5):
    P = ctx.prog
    st = ctx.stats
    sq = ctx.sq
    xn = ctx.xn
    for tc in range(NT):
        tile = ctx.X[:, tc * D:(tc + 1) * D]
        rsrc = [x_tile_region(tc)]
        P.emit("scalar", lambda e, t=tile: e.square(sq[:, :], t),
               reads=rsrc, writes=["sq"])
        P.emit("vector", lambda e: e.reduce_sum(st[:, 1:2], sq[:, :], AX),
               reads=["sq"], writes=["st"])
        P.emit("vector", lambda e: e.tensor_scalar(
            st[:, 3:4], st[:, 1:2], 1.0 / D, float(eps), ALU.mult, ALU.add),
               reads=["st"], writes=["st"])
        P.emit("scalar", lambda e: e.activation(st[:, 4:5], st[:, 3:4], AF.Sqrt),
               reads=["st"], writes=["st"])
        P.emit("vector", lambda e: e.reciprocal(st[:, 4:5], st[:, 4:5]),
               reads=["st"], writes=["st"])
        P.emit("vector", lambda e, t=tile: e.tensor_scalar(
            sq[:, :], t, st[:, 4:5], None, ALU.mult),
               reads=rsrc + ["st"], writes=["sq"])
        P.emit("vector", lambda e: e.tensor_tensor(
            xn[:, :], sq[:, :], wbc_ap, ALU.mult),
               reads=["sq", wbc_reg], writes=["xn"])

        ptb = ctx.ps_pt_bf if tc % 2 == 0 else ctx.ps_ctx_bf
        pregs = ["ps_ptA", "ps_ptB"] if tc % 2 == 0 else ["ps_ctx"]

        def tr_fn(e, ptb=ptb):
            ins = None
            for c in range(NC):
                ins = e.transpose(ptb[:, c * 128:(c + 1) * 128],
                                  xn[:, c * 128:(c + 1) * 128], ctx.identity)
            return ins
        P.emit("tensor", tr_fn, reads=["xn", "ident"], writes=pregs)
        P.emit("vector", lambda e, t=tc, ptb=ptb: e.tensor_copy(
            ctx.xnT_h3[:, :, t * 128:t * 128 + 128],
            ptb.rearrange("p (c t) -> p c t", c=NC)),
               reads=pregs, writes=[f"xnT{tc // 4}"])


def emit_l0_ffn(ctx, seq):
    P = ctx.prog
    pool = ctx.pool
    P.barrier()
    pool.push()
    tab = pool.carve(128, 2 * S, BF16, "tab")
    wln1 = pool.carve(128, D, BF16, "wln1")
    wln2 = pool.carve(128, D, BF16, "wln2")
    P.dma("gpsimd", lambda e: e.dma_start(tab[:, :], ctx.d_c["c_tabB"][:, :]),
          writes=["tab"])
    P.dma("gpsimd", lambda e: e.dma_start(ctx.perm[:, :], ctx.d_c["c_permB"][:, :]),
          writes=["perm"])
    P.dma("sync", lambda e: e.dma_start(
        wln1[:, :], ctx.d_c["c_l0f_wln1"][:, :].to_broadcast((128, D))),
          writes=["wln1"])
    P.dma("sync", lambda e: e.dma_start(
        wln2[:, :], ctx.d_c["c_l0f_wln2"][:, :].to_broadcast((128, D))),
          writes=["wln2"])

    st = ctx.stats
    sq = ctx.sq
    xn = ctx.xn
    for tc in range(NT):
        tile = ctx.X[:, tc * D:(tc + 1) * D]
        rsrc = [x_tile_region(tc)]
        P.dma("sync", lambda e, tc=tc: e.dma_start(
            ctx.d_spill[:, tc * D:(tc + 1) * D], ctx.X[:, tc * D:(tc + 1) * D]),
              reads=rsrc)
        P.emit("vector", lambda e, t=tile: e.reduce_sum(st[:, 0:1], t, AX),
               reads=rsrc, writes=["st"])
        P.emit("scalar", lambda e, t=tile: e.square(sq[:, :], t),
               reads=rsrc, writes=["sq"])
        P.emit("vector", lambda e: e.reduce_sum(st[:, 1:2], sq[:, :], AX),
               reads=["sq"], writes=["st"])
        P.emit("vector", lambda e: e.tensor_scalar(
            st[:, 2:3], st[:, 0:1], 1.0 / D, None, ALU.mult),
               reads=["st"], writes=["st"])
        P.emit("vector", lambda e: e.tensor_tensor(
            st[:, 3:4], st[:, 2:3], st[:, 2:3], ALU.mult),
               reads=["st"], writes=["st"])
        P.emit("vector", lambda e: e.scalar_tensor_tensor(
            st[:, 3:4], st[:, 1:2], 1.0 / D, st[:, 3:4], ALU.mult, ALU.subtract),
               reads=["st"], writes=["st"])
        P.emit("vector", lambda e: e.tensor_scalar(
            st[:, 3:4], st[:, 3:4], 1e-5, None, ALU.add),
               reads=["st"], writes=["st"])
        P.emit("scalar", lambda e: e.activation(st[:, 4:5], st[:, 3:4], AF.Sqrt),
               reads=["st"], writes=["st"])
        P.emit("vector", lambda e: e.reciprocal(st[:, 4:5], st[:, 4:5]),
               reads=["st"], writes=["st"])
        P.emit("vector", lambda e, t=tile: e.tensor_scalar(
            t, t, st[:, 2:3], st[:, 4:5], ALU.subtract, ALU.mult),
               reads=rsrc + ["st"], writes=rsrc)
        P.emit("scalar", lambda e, t=tile: e.square(sq[:, :], t),
               reads=rsrc, writes=["sq"])
        P.emit("vector", lambda e: e.reduce_sum(st[:, 5:6], sq[:, :], AX),
               reads=["sq"], writes=["st"])
        P.emit("vector", lambda e: e.tensor_scalar(
            st[:, 5:6], st[:, 5:6], 1.0 / D, 1e-5, ALU.mult, ALU.add),
               reads=["st"], writes=["st"])
        P.emit("scalar", lambda e: e.activation(st[:, 6:7], st[:, 5:6], AF.Sqrt),
               reads=["st"], writes=["st"])
        P.emit("vector", lambda e: e.reciprocal(st[:, 6:7], st[:, 6:7]),
               reads=["st"], writes=["st"])
        P.emit("vector", lambda e, t=tile: e.tensor_scalar(
            sq[:, :], t, st[:, 6:7], None, ALU.mult),
               reads=rsrc + ["st"], writes=["sq"])
        P.emit("vector", lambda e: e.tensor_tensor(
            xn[:, :], sq[:, :], wln1[:, :], ALU.mult),
               reads=["sq", "wln1"], writes=["xn"])

        ptb = ctx.ps_pt_bf if tc % 2 == 0 else ctx.ps_ctx_bf
        pregs = ["ps_ptA", "ps_ptB"] if tc % 2 == 0 else ["ps_ctx"]

        def tr_fn(e, ptb=ptb):
            ins = None
            for c in range(NC):
                ins = e.transpose(ptb[:, c * 128:(c + 1) * 128],
                                  xn[:, c * 128:(c + 1) * 128], ctx.identity)
            return ins
        P.emit("tensor", tr_fn, reads=["xn", "ident"], writes=pregs)
        P.emit("vector", lambda e, t=tc, ptb=ptb: e.tensor_copy(
            ctx.xnT_h3[:, :, t * 128:t * 128 + 128],
            ptb.rearrange("p (c t) -> p c t", c=NC)),
               reads=pregs, writes=[f"xnT{tc // 4}"])

    pool.push()
    kvfm = pool.carve(128, 4 * S, BF16, "kvfm")
    kvtm = pool.carve(128, NT * 512, BF16, "kvtm")
    ctxb4 = pool.carve(128, 4 * S, BF16, "ctxb4")
    KA = pool.carve(128, S, BF16, "KAf")
    KB = pool.carve(128, S, BF16, "KBf")
    pool.push()
    ckv = pool.carve(128, 2 * S, BF16, "ckv")
    wkva = pool.carve(128, 8 * 256, BF16, "wkva")
    wkvb = pool.carve(128, 512, BF16, "wkvb")
    P.dma("gpsimd", lambda e: e.dma_start(wkva[:, :], ctx.d_c["c_l0f_wkva"][:, :]),
          writes=["wkva"])
    P.dma("gpsimd", lambda e: e.dma_start(wkvb[:, :], ctx.d_c["c_l0f_wkvb"][:, :]),
          writes=["wkvb"])
    for g in range(4):
        gs = slice(g * 512, (g + 1) * 512)
        for mchunk in range(2):
            ps = ctx.ps_y[:, 0:512]

            def mm_fn(e, g=g, m=mchunk, ps=ps):
                ins = None
                for c in range(NC):
                    ins = e.matmul(ps, wkva[:, c * 256 + m * 128: c * 256 + (m + 1) * 128],
                                   ctx.xnT[:, c * S + g * 512: c * S + (g + 1) * 512],
                                   start=(c == 0), stop=(c == NC - 1))
                return ins
            P.emit("tensor", mm_fn, reads=["wkva", f"xnT{g}"], writes=["ps_y0"])
            P.emit("scalar", lambda e, gs=gs, m=mchunk, ps=ps: e.copy(
                ckv[:, m * S + gs.start: m * S + gs.stop], ps),
                   reads=["ps_y0"], writes=["ckv"])
    P.emit("vector", lambda e: e.memset(ctx.t2[:, :], 0.0), writes=["t2"])
    P.dma("sync", lambda e: e.dma_start(KA[32:64, :], ckv[0:32, S:2 * S]),
          reads=["ckv"], writes=["KAf"])
    for g in range(4):
        gs = slice(g * 512, (g + 1) * 512)

        def rope_k(e, gs=gs):
            e.tensor_tensor(ctx.t1[32:64, :], KA[32:64, gs],
                            tab[32:64, gs], ALU.mult)
            gs2 = slice(S + gs.start, S + gs.stop)
            return e.tensor_tensor(ctx.t2[32:64, :], KA[32:64, gs],
                                   tab[32:64, gs2], ALU.mult)
        P.emit("vector", rope_k, reads=["KAf", "tab"], writes=["t1", "t2"])
        P.emit("tensor", lambda e: e.matmul(ctx.ps_y[:, 512:1024], ctx.perm[:, :],
                                            ctx.t2[:, :], start=True, stop=True),
               reads=["t2", "perm"], writes=["ps_y1"])
        P.emit("vector", lambda e, gs=gs: e.tensor_tensor(
            KA[32:64, gs], ctx.t1[32:64, :], ctx.ps_y[32:64, 512:1024], ALU.add),
               reads=["t1", "ps_y1"], writes=["KAf"])
    for n, (m, g) in enumerate((m, g) for m in range(4) for g in range(4)):
        ys = n % 2
        ps = ctx.ps_y[:, ys * 512:(ys + 1) * 512]
        P.emit("tensor", lambda e, m=m, g=g, ps=ps: e.matmul(
            ps, wkvb[:, m * 128:(m + 1) * 128],
            ckv[:, g * 512:(g + 1) * 512], start=True, stop=True),
               reads=["wkvb", "ckv"], writes=[f"ps_y{ys}"])
        eng = "scalar" if ys == 0 else "vector"
        fn = (lambda e, m=m, g=g, ps=ps: e.copy(
            kvfm[:, m * S + g * 512: m * S + (g + 1) * 512], ps)) if ys == 0             else (lambda e, m=m, g=g, ps=ps: e.tensor_copy(
                kvfm[:, m * S + g * 512: m * S + (g + 1) * 512], ps))
        P.emit(eng, fn, reads=[f"ps_y{ys}"], writes=["kvfm"])
    for tc in range(NT):
        ys = tc % 2
        ps = ctx.ps_y[:, ys * 512:(ys + 1) * 512]
        P.emit("tensor", lambda e, tc=tc, ps=ps: e.matmul(
            ps, ckv[:, tc * 128:(tc + 1) * 128], wkvb[:, :],
            start=True, stop=True),
               reads=["wkvb", "ckv"], writes=[f"ps_y{ys}"])
        eng = "scalar" if ys == 0 else "vector"
        fn = (lambda e, tc=tc, ps=ps: e.copy(
            kvtm[:, tc * 512:(tc + 1) * 512], ps)) if ys == 0             else (lambda e, tc=tc, ps=ps: e.tensor_copy(
                kvtm[:, tc * 512:(tc + 1) * 512], ps))
        P.emit(eng, fn, reads=[f"ps_y{ys}"], writes=["kvtm"])
    P.barrier()
    pool.pop()

    pool.push()
    PbA = pool.carve(128, S, BF16, "PbA")
    PbB = pool.carve(128, S, BF16, "PbB")
    PbA2 = ctx.sq[:, :].bitcast(BF16)
    PbB2 = pool.carve(128, S, BF16, "PbB2")
    ctx.PT2 = [pool.carve(128, 512, BF16, f"PT2F_{i}") for i in range(2)]
    QpF = [pool.carve(128, S, BF16, f"QpF{i}") for i in range(2)]
    wqF = [pool.carve(128, 1024, BF16, f"wqF{i}") for i in range(2)]
    P.emit("vector", lambda e: e.memset(KA[64:128, :], 0.0), writes=["KAf"])
    P.emit("vector", lambda e: e.memset(KB[0:64, :], 0.0), writes=["KBf"])
    P.dma("gpsimd", lambda e: e.dma_start(KB[96:128, :], KA[32:64, :]),
          reads=["KAf"], writes=["KBf"])

    def proj_pairs_l0f(q):
        for i in range(2):
            P.dma("gpsimd", lambda e, q=q, i=i: e.dma_start(
                wqF[i][:, :], ctx.d_c["c_l0f_wq"][2 * q + i]), writes=[f"wqF{i}"])
        tasks = [(i, g) for g in range(4) for i in range(2)]

        def emit_tail(prev):
            i, g, ys, tp = prev
            gs = slice(g * 512, (g + 1) * 512)
            P.emit("tensor", lambda e, tp=tp: e.matmul(
                ctx.ps_ctx[:, 0:512], ctx.perm[:, :], ctx.T2[tp][:, :],
                start=True, stop=True),
                   reads=[f"t2_{tp}", "perm"], writes=["ps_ctx"])
            P.emit("vector", lambda e, gs=gs, i=i, tp=tp: e.tensor_tensor(
                QpF[i][:, gs], ctx.T1[tp][:, :], ctx.ps_ctx[:, 0:512], ALU.add),
                   reads=[f"t1_{tp}", "ps_ctx"], writes=[f"QpF{i}"])

        prev = None
        for n, (i, g) in enumerate(tasks):
            ys = n % 2
            tp = n % 2
            ps = ctx.ps_y[:, ys * 512:(ys + 1) * 512]

            def mm_fn(e, g=g, i=i, ps=ps):
                ins = None
                for c in range(NC):
                    ins = e.matmul(ps, wqF[i][:, c * 128:(c + 1) * 128],
                                   ctx.xnT[:, c * S + g * 512: c * S + (g + 1) * 512],
                                   start=(c == 0), stop=(c == NC - 1))
                return ins
            P.emit("tensor", mm_fn, reads=[f"wqF{i}", f"xnT{g}"],
                   writes=[f"ps_y{ys}"])
            if prev is not None:
                emit_tail(prev)
            gs = slice(g * 512, (g + 1) * 512)

            def rope1_fn(e, gs=gs, ps=ps, tp=tp):
                e.tensor_tensor(ctx.T1[tp][:, :], ps, tab[:, gs], ALU.mult)
                gs2 = slice(S + gs.start, S + gs.stop)
                return e.tensor_tensor(ctx.T2[tp][:, :], ps, tab[:, gs2], ALU.mult)
            P.emit("vector", rope1_fn, reads=[f"ps_y{ys}", "tab"],
                   writes=[f"t1_{tp}", f"t2_{tp}"])
            prev = (i, g, ys, tp)
        emit_tail(prev)

    pendq = []
    ui = 0
    for q in range(4):
        proj_pairs_l0f(q)
        for j in range(2):
            hA = 4 * q + 2 * j
            hB = hA + 1
            P.dma("gpsimd", lambda e, hA=hA: e.dma_start(
                KA[0:32, :], kvfm[32 * (hA % 4): 32 * (hA % 4) + 32,
                                  (hA // 4) * S:(hA // 4 + 1) * S]),
                  reads=["kvfm"], writes=["KAf"])
            P.dma("gpsimd", lambda e, hB=hB: e.dma_start(
                KB[64:96, :], kvfm[32 * (hB % 4): 32 * (hB % 4) + 32,
                                   (hB // 4) * S:(hB // 4 + 1) * S]),
                  reads=["kvfm"], writes=["KBf"])
            for qt in range(NT):
                upar = ui % 2
                pbA = PbA if upar == 0 else PbA2
                pbB = PbB if upar == 0 else PbB2
                members = [
                    {"q_ap": QpF[j][:, qt * 128: qt * 128 + 128],
                     "q_reads": [f"QpF{j}"], "k_ap": KA, "k_reads": ["KAf"],
                     "pb": pbA, "pbreg": f"PbA{upar}", "fcol": upar * 64},
                    {"q_ap": QpF[j][:, qt * 128: qt * 128 + 128],
                     "q_reads": [f"QpF{j}"], "k_ap": KB, "k_reads": ["KBf"],
                     "pb": pbB, "pbreg": f"PbB{upar}", "fcol": upar * 64 + 16}]

                def evac(q=q, j=j, qt=qt):
                    r0 = 64 * j
                    P.emit("scalar", lambda e, r0=r0, q=q, qt=qt: e.copy(
                        ctxb4[r0:r0 + 32, q * S + qt * 128: q * S + qt * 128 + 128],
                        ctx.ps_ctx[r0:r0 + 32, 0:128]),
                           reads=["ps_ctx"], writes=[f"ctxb4_{q}"])
                    P.emit("scalar", lambda e, r0=r0, q=q, qt=qt: e.copy(
                        ctxb4[r0 + 32:r0 + 64, q * S + qt * 128: q * S + qt * 128 + 128],
                        ctx.ps_ctx[r0 + 32:r0 + 64, 128:256]),
                           reads=["ps_ctx"], writes=[f"ctxb4_{q}"])
                av = (lambda kc, q=q: kvtm[:, kc * 512 + q * 128:
                                           kc * 512 + q * 128 + 128])
                stA = flash_phase1(ctx, qt, members[0:1])
                if len(pendq) >= 2:
                    flash_trailing(ctx, *pendq.pop(0))
                flash_phase2(ctx, qt, members[0:1], stA, eps=None)
                stB = flash_phase1(ctx, qt, members[1:2])
                flash_phase2(ctx, qt, members[1:2], stB, eps=None)
                pendq.append((qt, members, av, ["kvtm"], evac))
                ui += 1
    for p in pendq:
        flash_trailing(ctx, *p)
    P.barrier()
    pool.pop()

    pool.push()
    wo = pool.carve(128, 4 * D, BF16, "wo")
    for c in range(4):
        P.dma("gpsimd", lambda e, c=c: e.dma_start(
            wo[:, c * D:(c + 1) * D], ctx.d_c["c_l0f_wo"][:, c * D:(c + 1) * D]),
              writes=["wo"])
    for qt in range(NT):
        for half in range(2):
            ps = ctx.ps_y[:, half * 512:(half + 1) * 512]

            def wo_fn(e, qt=qt, half=half, ps=ps):
                ins = None
                for c in range(4):
                    ins = e.matmul(ps, ctxb4[:, c * S + qt * 128: c * S + qt * 128 + 128],
                                   wo[:, c * D + half * 512: c * D + (half + 1) * 512],
                                   start=(c == 0), stop=(c == 3))
                return ins
            P.emit("tensor", wo_fn,
                   reads=["wo", "ctxb4_0", "ctxb4_1", "ctxb4_2", "ctxb4_3"],
                   writes=[f"ps_y{half}"])
            P.emit("vector", lambda e, qt=qt, half=half, ps=ps: e.tensor_tensor(
                ctx.X[:, qt * D + half * 512: qt * D + (half + 1) * 512],
                ctx.X[:, qt * D + half * 512: qt * D + (half + 1) * 512],
                ps, ALU.add),
                   reads=[f"ps_y{half}", x_tile_region(qt)],
                   writes=[x_tile_region(qt)])
    P.barrier()
    pool.pop()
    pool.pop()

    emit_rms_to_xnT(ctx, wln2[:, :], "wln2")

    pool.push()
    wgate = pool.carve(128, 64, BF16, "wgate")
    ebias = pool.carve(128, 8, F32, "ebias")
    comb = pool.carve(128, NT * 8, F32, "comb")
    lg = pool.carve(128, 16, F32, "lg")
    wg_e2 = [pool.carve(128, 8 * 512, BF16, f"wg_e{i}") for i in range(2)]
    wu_e2 = [pool.carve(128, 8 * 512, BF16, f"wu_e{i}") for i in range(2)]
    wd_e2 = [pool.carve(128, 4 * D, BF16, f"wd_e{i}") for i in range(2)]
    g_s = pool.carve(128, 4 * S, BF16, "g_s")
    h_fm = pool.carve(128, 4 * S, BF16, "h_fm")
    P.dma("gpsimd", lambda e: e.dma_start(wgate[:, :], ctx.d_c["c_l0f_wgate"][:, :]),
          writes=["wgate"])
    P.dma("sync", lambda e: e.dma_start(
        ebias[:, :], ctx.d_c["c_l0f_ebias"][:, :].to_broadcast((128, 8))),
          writes=["ebias"])
    for tc in range(NT):
        ps = ctx.ps_y[:, 0:8]

        def gate_fn(e, tc=tc, ps=ps):
            ins = None
            for c in range(NC):
                ins = e.matmul(ps, ctx.xnT[:, c * S + tc * 128: c * S + tc * 128 + 128],
                               wgate[:, c * 8:(c + 1) * 8],
                               start=(c == 0), stop=(c == NC - 1))
            return ins
        P.emit("tensor", gate_fn, reads=["wgate", f"xnT{tc // 4}"], writes=["ps_y0"])
        st2 = ctx.stats2
        P.emit("vector", lambda e, ps=ps: e.tensor_tensor(
            lg[:, 0:8], ps, ebias[:, :], ALU.add),
               reads=["ps_y0", "ebias"], writes=["lg"])
        P.emit("vector", lambda e: e.reduce_max(st2[:, 4:5], lg[:, 0:8], AX,
                                                negate=True),
               reads=["lg"], writes=["st2"])
        P.emit("scalar", lambda e: e.activation(lg[:, 0:8], lg[:, 0:8], AF.Exp,
                                                bias=st2[:, 4:5], scale=1.0),
               reads=["lg", "st2"], writes=["lg"])
        P.emit("vector", lambda e: e.reduce_max(st2[:, 5:6], lg[:, 0:8], AX),
               reads=["lg"], writes=["st2"])
        P.emit("vector", lambda e: e.tensor_scalar(
            lg[:, 8:16], lg[:, 0:8], st2[:, 5:6], None, ALU.is_lt),
               reads=["lg", "st2"], writes=["lg"])
        P.emit("vector", lambda e: e.tensor_tensor(
            lg[:, 8:16], lg[:, 0:8], lg[:, 8:16], ALU.mult),
               reads=["lg"], writes=["lg"])
        P.emit("vector", lambda e: e.reduce_max(st2[:, 6:7], lg[:, 8:16], AX),
               reads=["lg"], writes=["st2"])
        P.emit("vector", lambda e: e.tensor_scalar(
            lg[:, 8:16], lg[:, 0:8], st2[:, 6:7], None, ALU.is_ge),
               reads=["lg", "st2"], writes=["lg"])
        P.emit("vector", lambda e: e.tensor_tensor(
            st2[:, 7:8], st2[:, 5:6], st2[:, 6:7], ALU.add),
               reads=["st2"], writes=["st2"])
        P.emit("vector", lambda e: e.reciprocal(st2[:, 7:8], st2[:, 7:8]),
               reads=["st2"], writes=["st2"])
        P.emit("vector", lambda e, tc=tc: e.scalar_tensor_tensor(
            comb[:, tc * 8:(tc + 1) * 8], lg[:, 0:8], st2[:, 7:8], lg[:, 8:16],
            ALU.mult, ALU.mult),
               reads=["lg", "st2"], writes=["comb"])
    for ei in range(9):
        if ei < 8:
            wgd = ctx.d_c["c_l0f_wg8"][ei]
            wud = ctx.d_c["c_l0f_wu8"][ei]
            wdd = ctx.d_c["c_l0f_wd8"][ei]
        else:
            wgd = ctx.d_c["c_l0f_swg"][:, :]
            wud = ctx.d_c["c_l0f_swu"][:, :]
            wdd = ctx.d_c["c_l0f_swd"][:, :]
        ep = ei % 2
        wg_e, wu_e, wd_e = wg_e2[ep], wu_e2[ep], wd_e2[ep]
        P.dma("gpsimd", lambda e, w=wgd, wg_e=wg_e: e.dma_start(wg_e[:, :], w),
              writes=[f"wg_e{ep}"])
        P.dma("gpsimd", lambda e, w=wud, wu_e=wu_e: e.dma_start(wu_e[:, :], w),
              writes=[f"wu_e{ep}"])
        P.dma("gpsimd", lambda e, w=wdd, wd_e=wd_e: e.dma_start(wd_e[:, :], w),
              writes=[f"wd_e{ep}"])
        for ic in range(4):
            for g in range(4):
                ps = ctx.ps_y[:, 0:512]
                ps2 = ctx.ps_y[:, 512:1024]

                def gmm(e, ic=ic, g=g, ps=ps, wg_e=wg_e):
                    ins = None
                    for c in range(NC):
                        ins = e.matmul(ps, wg_e[:, c * 512 + ic * 128: c * 512 + (ic + 1) * 128],
                                       ctx.xnT[:, c * S + g * 512: c * S + (g + 1) * 512],
                                       start=(c == 0), stop=(c == NC - 1))
                    return ins
                P.emit("tensor", gmm, reads=[f"wg_e{ep}", f"xnT{g}"], writes=["ps_y0"])
                P.emit("scalar", lambda e, ic=ic, g=g, ps=ps: e.activation(
                    g_s[:, ic * S + g * 512: ic * S + (g + 1) * 512], ps, AF.Silu),
                       reads=["ps_y0"], writes=["g_s"])

                def umm(e, ic=ic, g=g, ps2=ps2, wu_e=wu_e):
                    ins = None
                    for c in range(NC):
                        ins = e.matmul(ps2, wu_e[:, c * 512 + ic * 128: c * 512 + (ic + 1) * 128],
                                       ctx.xnT[:, c * S + g * 512: c * S + (g + 1) * 512],
                                       start=(c == 0), stop=(c == NC - 1))
                    return ins
                P.emit("tensor", umm, reads=[f"wu_e{ep}", f"xnT{g}"], writes=["ps_y1"])
                P.emit("vector", lambda e, ic=ic, g=g, ps2=ps2: e.tensor_tensor(
                    h_fm[:, ic * S + g * 512: ic * S + (g + 1) * 512], ps2,
                    g_s[:, ic * S + g * 512: ic * S + (g + 1) * 512], ALU.mult),
                       reads=["ps_y1", "g_s"], writes=["h_fm"])
        for tc in range(NT):
            for half in range(2):
                ps = ctx.ps_y[:, half * 512:(half + 1) * 512]

                def dmm(e, tc=tc, half=half, ps=ps, wd_e=wd_e):
                    ins = None
                    for ic in range(4):
                        ins = e.matmul(ps, h_fm[:, ic * S + tc * 128: ic * S + tc * 128 + 128],
                                       wd_e[:, ic * D + half * 512: ic * D + (half + 1) * 512],
                                       start=(ic == 0), stop=(ic == 3))
                    return ins
                P.emit("tensor", dmm, reads=["h_fm", f"wd_e{ep}"], writes=[f"ps_y{half}"])
                if ei < 8:
                    P.emit("vector", lambda e, tc=tc, half=half, ps=ps, ei=ei:
                           e.scalar_tensor_tensor(
                               ctx.X[:, tc * D + half * 512: tc * D + (half + 1) * 512],
                               ps, comb[:, tc * 8 + ei: tc * 8 + ei + 1],
                               ctx.X[:, tc * D + half * 512: tc * D + (half + 1) * 512],
                               ALU.mult, ALU.add),
                           reads=[f"ps_y{half}", "comb", x_tile_region(tc)],
                           writes=[x_tile_region(tc)])
                else:
                    P.emit("vector", lambda e, tc=tc, half=half, ps=ps: e.tensor_tensor(
                        ctx.X[:, tc * D + half * 512: tc * D + (half + 1) * 512],
                        ctx.X[:, tc * D + half * 512: tc * D + (half + 1) * 512],
                        ps, ALU.add),
                           reads=[f"ps_y{half}", x_tile_region(tc)],
                           writes=[x_tile_region(tc)])
    P.barrier()
    pool.pop()

    xres = ctx.sq
    for tc in range(NT):
        P.dma("sync", lambda e, tc=tc: e.dma_start(
            xres[:, :], ctx.d_spill[:, tc * D:(tc + 1) * D]),
              writes=["sq"])
        P.emit("vector", lambda e, tc=tc: e.tensor_tensor(
            ctx.X[:, tc * D:(tc + 1) * D], ctx.X[:, tc * D:(tc + 1) * D],
            xres[:, :], ALU.add),
               reads=["sq", x_tile_region(tc)], writes=[x_tile_region(tc)])
    P.barrier()
    pool.pop()


def emit_l1_attn(ctx, seq):
    P = ctx.prog
    pool = ctx.pool
    P.barrier()
    pool.push()
    MhT = pool.carve(128, 16 * 128, BF16, "MhT")
    veff = pool.carve(128, D, BF16, "veff")
    qlat = pool.carve(128, S, BF16, "qlat")
    ckvf = pool.carve(128, S, BF16, "ckvf")
    ckvt = pool.carve(128, NT * 128, BF16, "ckvt")
    ctx.PT2 = [pool.carve(128, 512, BF16, f"PT2L_{i}") for i in range(2)]
    ctxs4 = [pool.carve(128, 128, BF16, f"ctxs{i}") for i in range(4)]
    pool.push()
    wa = pool.carve(128, 8 * 128, BF16, "wa")
    wb = pool.carve(128, 8 * 128, BF16, "wb")
    keff = pool.carve(128, 8 * 128, BF16, "keff")
    hpad2 = [pool.carve(128, 128, BF16, f"hpad{i}") for i in range(2)]
    hpad = hpad2[0]
    wot = pool.carve(128, 8 * D, BF16, "wot")
    if L1A_PREPMODE == 2:
        P.barrier()
        pool.pop()
        pool.pop()
        return
    P.dma("gpsimd", lambda e: e.dma_start(wa[:, :], ctx.d_c["c_l1a_wuq"][:, :]),
          writes=["wa"])
    P.dma("gpsimd", lambda e: e.dma_start(wb[:, :], ctx.d_c["c_l1a_wuk"][:, :]),
          writes=["wb"])
    if L1A_PREPMODE == 1:
        P.barrier()
        pool.pop()
        pool.pop()
        return
    ps = ctx.ps_ctx[0:128, 0:128]

    def a_fn(e):
        ins = None
        for c in range(NC):
            ins = e.matmul(ps, wa[:, c * 128:(c + 1) * 128],
                           wb[:, c * 128:(c + 1) * 128],
                           start=(c == 0), stop=(c == NC - 1))
        return ins
    P.emit("tensor", a_fn, reads=["wa", "wb"], writes=["ps_ctx"])
    P.emit("scalar", lambda e: e.copy(hpad[:, :], ps), reads=["ps_ctx"],
           writes=["hpad0"])
    P.dma("gpsimd", lambda e: e.dma_start(wb[:, 0:D], ctx.d_c["c_l1a_wdqraw"][:, :]),
          writes=["wb"])
    for m in range(NC):
        P.emit("tensor", lambda e, m=m: e.matmul(
            ps, wb[:, m * 128:(m + 1) * 128], hpad[:, :], start=True, stop=True),
               reads=["wb", "hpad0"], writes=["ps_ctx"])
        P.emit("scalar", lambda e, m=m: e.copy(keff[:, m * 128:(m + 1) * 128], ps),
               reads=["ps_ctx"], writes=["keff"])
    for h in range(16):
        r0 = 64 * (h % 2)
        ro = 64 - r0
        hp = hpad2[h % 2]
        hreg = f"hpad{h % 2}"
        P.emit("vector", lambda e, ro=ro, hp=hp: e.memset(hp[ro:ro + 64, :], 0.0),
               writes=[hreg])
        P.emit("vector", lambda e, h=h, r0=r0, hp=hp: e.tensor_copy(
            hp[r0:r0 + 64, :], keff[r0:r0 + 64, (h // 2) * 128:(h // 2 + 1) * 128]),
               reads=["keff"], writes=[hreg])
        psm = ctx.ps_y[:, (h % 2) * 512:(h % 2) * 512 + 128]
        P.emit("tensor", lambda e, h=h, hp=hp, psm=psm: e.matmul(
            psm, hp[:, :], wa[:, (h // 2) * 128:(h // 2 + 1) * 128],
            start=True, stop=True),
               reads=["wa", hreg], writes=[f"ps_y{h % 2}"])
        eng = "scalar" if h % 2 == 0 else "vector"
        fn = (lambda e, h=h, psm=psm: e.copy(MhT[:, h * 128:(h + 1) * 128], psm)) \
            if h % 2 == 0 else \
            (lambda e, h=h, psm=psm: e.tensor_copy(MhT[:, h * 128:(h + 1) * 128], psm))
        P.emit(eng, fn, reads=[f"ps_y{h % 2}"], writes=["MhT"])
    P.dma("gpsimd", lambda e: e.dma_start(wa[:, :], ctx.d_c["c_l1a_wuv"][:, :]),
          writes=["wa"])
    for c in range(NC):
        P.dma("gpsimd", lambda e, c=c: e.dma_start(
            wot[:, c * D:(c + 1) * D], ctx.d_c["c_l1a_wot"][:, c * D:(c + 1) * D]),
              writes=["wot"])
    for half in range(2):
        psv = ctx.ps_y[:, half * 512:(half + 1) * 512]

        def v_fn(e, half=half, psv=psv):
            ins = None
            for c in range(NC):
                ins = e.matmul(psv, wa[:, c * 128:(c + 1) * 128],
                               wot[:, c * D + half * 512: c * D + (half + 1) * 512],
                               start=(c == 0), stop=(c == NC - 1))
            return ins
        P.emit("tensor", v_fn, reads=["wa", "wot"], writes=[f"ps_y{half}"])
        P.emit("scalar", lambda e, half=half, psv=psv: e.copy(
            veff[:, half * 512:(half + 1) * 512], psv),
               reads=[f"ps_y{half}"], writes=["veff"])
    P.barrier()
    pool.pop()
    if L1A_TRUNC == 1:
        pool.pop()
        return

    emit_layernorm(ctx, ctx.X, x_tile_region)
    pool.push()
    wdq = pool.carve(128, 8 * 128, BF16, "wdq")
    wdkv = pool.carve(128, 8 * 128, BF16, "wdkv")
    P.dma("gpsimd", lambda e: e.dma_start(wdq[:, :], ctx.d_c["c_l1a_wdq"][:, :]),
          writes=["wdq"])
    P.dma("gpsimd", lambda e: e.dma_start(wdkv[:, :], ctx.d_c["c_l1a_wdkv"][:, :]),
          writes=["wdkv"])
    for g in range(4):
        for (w, wr, dst, dr) in ((wdq, "wdq", qlat, "qlat"), (wdkv, "wdkv", ckvf, "ckvf")):
            psq = ctx.ps_y[:, 0:512]

            def q_fn(e, g=g, w=w, psq=psq):
                ins = None
                for c in range(NC):
                    ins = e.matmul(psq, w[:, c * 128:(c + 1) * 128],
                                   ctx.xnT[:, c * S + g * 512: c * S + (g + 1) * 512],
                                   start=(c == 0), stop=(c == NC - 1))
                return ins
            P.emit("tensor", q_fn, reads=[wr, f"xnT{g}"], writes=["ps_y0"])
            P.emit("scalar", lambda e, g=g, dst=dst, psq=psq: e.copy(
                dst[:, g * 512:(g + 1) * 512], psq),
                   reads=["ps_y0"], writes=[dr])
    if L1A_TRUNC == 2:
        P.barrier()
        pool.pop()
        pool.pop()
        return
    if L1A_EARLYDUMP and seq == 0:
        P.barrier()
        for nm, ap in (("e_qlat", qlat), ("e_ckvf", ckvf), ("e_xnT", ctx.xnT),
                       ("e_X", ctx.X)):
            p, n = ap.shape
            d = ctx.nc.dram_tensor(f"dbg_{nm}", [p, n], ap.dtype,
                                   kind="ExternalOutput")
            P.dma("gpsimd", lambda e, d=d, ap=ap: e.dma_start(d[:, :], ap),
                  writes=[f"dbgw_{nm}"])
        P.barrier()

    for tc in range(NT):
        ptb = ctx.ps_pt_bf if tc % 2 == 0 else ctx.ps_ctx_bf
        preg = "ps_ptA" if tc % 2 == 0 else "ps_ctx"
        P.emit("tensor", lambda e, tc=tc, ptb=ptb: e.transpose(
            ptb[:, 0:128], ckvf[:, tc * 128:(tc + 1) * 128], ctx.identity),
               reads=["ckvf", "ident"], writes=[preg])
        eng = "scalar" if tc % 2 == 0 else "vector"
        fn = (lambda e, tc=tc, ptb=ptb: e.copy(
            ckvt[:, tc * 128:(tc + 1) * 128], ptb[:, 0:128])) if tc % 2 == 0 \
            else (lambda e, tc=tc, ptb=ptb: e.tensor_copy(
                ckvt[:, tc * 128:(tc + 1) * 128], ptb[:, 0:128]))
        P.emit(eng, fn, reads=[preg], writes=["ckvt"])
    if L1A_TRUNC == 3:
        P.barrier()
        pool.pop()
        pool.pop()
        return

    Keff = pool.carve(128, 16 * S, BF16, "Keff")
    for h in range(16):
        for g2 in range(4):
            ys = (h * 4 + g2) % 2
            ps2 = ctx.ps_y[:, ys * 512:(ys + 1) * 512]
            P.emit("tensor", lambda e, h=h, g2=g2, ps2=ps2: e.matmul(
                ps2, MhT[:, h * 128:(h + 1) * 128],
                ckvf[:, g2 * 512:(g2 + 1) * 512], start=True, stop=True),
                   reads=["MhT", "ckvf"], writes=[f"ps_y{ys}"])
            dst = Keff[:, h * S + g2 * 512: h * S + (g2 + 1) * 512]
            if (h * 4 + g2) % 2 == 0:
                P.emit("scalar", lambda e, dst=dst, ps2=ps2: e.copy(dst, ps2),
                       reads=[f"ps_y{ys}"], writes=[f"Keff{h // 4}"])
            else:
                P.emit("vector", lambda e, dst=dst, ps2=ps2: e.tensor_copy(dst, ps2),
                       reads=[f"ps_y{ys}"], writes=[f"Keff{h // 4}"])

    PbL = [ctx.xnT[:, i * S:(i + 1) * S] for i in range(8)]
    ctx.l1_qlat = qlat
    ctx.l1_ckvf = ckvf
    ctx.l1_ckvt = ckvt
    ctx.l1_Keff = Keff
    ctx.l1_MhT = MhT
    ctx.l1_veff = veff
    P.barrier()
    if L1A_TRUNC == 4:
        pool.pop()
        pool.pop()
        return

    pendq = []
    ui = 0
    for qt in range(NT):
        for grp in range(4):
            upar = ui % 2
            members = []
            for mi in range(4):
                h = grp * 4 + mi
                pslot = upar * 4 + mi
                members.append({
                    "q_ap": qlat[:, qt * 128: qt * 128 + 128],
                    "q_reads": ["qlat"],
                    "k_ap": Keff[:, h * S:(h + 1) * S],
                    "k_reads": [f"Keff{h // 4}"],
                    "pb": PbL[pslot], "pbreg": f"PbL{pslot}",
                    "fcol": upar * 64 + mi * 16})
            for mi in range(4):
                st = flash_phase1(ctx, qt, members[mi:mi + 1])
                if L1A_DEFER and mi == 0 and len(pendq) >= 2:
                    flash_trailing(ctx, *pendq.pop(0))
                flash_phase2(ctx, qt, members[mi:mi + 1], st, eps=None,
                             scale_pb=L1A_SCALEPB)

            def evac(qt=qt, grp=grp, upar=upar):
                for mi in range(4):
                    P.emit("scalar", lambda e, mi=mi: e.copy(
                        ctxs4[mi][:, :], ctx.ps_ctx[:, mi * 128:(mi + 1) * 128]),
                           reads=["ps_ctx"], writes=[f"ctxs{mi}"])

                def vf(e, grp=grp):
                    ins = None
                    for mi in range(4):
                        h = grp * 4 + mi
                        ins = e.matmul(ctx.ps_pt[:, mi * 64:(mi + 1) * 64],
                                       ctxs4[mi][:, :],
                                       veff[:, h * 64:(h + 1) * 64],
                                       start=True, stop=True,
                                       skip_group_check=True)
                    return ins
                P.emit("tensor", vf,
                       reads=["ctxs0", "ctxs1", "ctxs2", "ctxs3", "veff"],
                       writes=["ps_ptA"])
                for mi in range(4):
                    h = grp * 4 + mi
                    fb = upar * 64 + mi * 16
                    if L1A_SCALEPB:
                        P.emit("vector", lambda e, qt=qt, h=h, mi=mi:
                               e.tensor_tensor(
                                   ctx.X[:, qt * D + h * 64: qt * D + (h + 1) * 64],
                                   ctx.X[:, qt * D + h * 64: qt * D + (h + 1) * 64],
                                   ctx.ps_pt[:, mi * 64:(mi + 1) * 64], ALU.add),
                               reads=["ps_ptA", x_tile_region(qt)],
                               writes=[x_tile_region(qt)])
                    else:
                        P.emit("vector", lambda e, qt=qt, h=h, mi=mi, fb=fb:
                               e.scalar_tensor_tensor(
                                   ctx.X[:, qt * D + h * 64: qt * D + (h + 1) * 64],
                                   ctx.ps_pt[:, mi * 64:(mi + 1) * 64],
                                   ctx.fst[:, fb + 10: fb + 11],
                                   ctx.X[:, qt * D + h * 64: qt * D + (h + 1) * 64],
                                   ALU.mult, ALU.add),
                               reads=["ps_ptA", f"fst{fb}", x_tile_region(qt)],
                               writes=[x_tile_region(qt)])
            pendq.append((qt, members,
                          lambda kc: ckvt[:, kc * 128:(kc + 1) * 128],
                          ["ckvt"], evac))
            if not L1A_DEFER:
                flash_trailing(ctx, *pendq.pop())
            ui += 1
    for p in pendq:
        flash_trailing(ctx, *p)
    P.barrier()
    pool.pop()
    pool.pop()


def emit_l1_ffn(ctx, seq):
    P = ctx.prog
    pool = ctx.pool
    emit_layernorm(ctx, ctx.X, x_tile_region)
    P.barrier()
    pool.push()
    wdT = pool.carve(128, 12 * D, BF16, "wdT")
    hfm = pool.carve(128, 12 * S, BF16, "hfm")
    wgsb2 = [pool.carve(128, D, BF16, f"wgsb{i}") for i in range(2)]
    wusb2 = [pool.carve(128, D, BF16, f"wusb{i}") for i in range(2)]
    gsb = pool.carve(128, S, BF16, "gsb")
    for H in range(L1F_HALVES):
        for j in range(L1F_JCN):
            jc = H * 12 + j
            P.dma("gpsimd", lambda e, jc=jc, j=j: e.dma_start(
                wdT[:, j * D:(j + 1) * D], ctx.d_c["c_l1f_wd"][jc]),
                  writes=["wdT"])
        if not L1F_GATES:
            P.emit("vector", lambda e: e.memset(hfm[:, :], 0.001), writes=["hfm"])
        for j in range(L1F_JCN if L1F_GATES else 0):
            jc = H * 12 + j
            jp = j % 2
            wgsb = wgsb2[jp]
            wusb = wusb2[jp]
            P.dma("gpsimd", lambda e, jc=jc, wgsb=wgsb: e.dma_start(
                wgsb[:, :], ctx.d_c["c_l1f_wg"][:, jc * D:(jc + 1) * D]),
                  writes=[f"wgsb{jp}"])
            P.dma("gpsimd", lambda e, jc=jc, wusb=wusb: e.dma_start(
                wusb[:, :], ctx.d_c["c_l1f_wu"][:, jc * D:(jc + 1) * D]),
                  writes=[f"wusb{jp}"])
            for g in range(4):
                ps = ctx.ps_y[:, 0:512]
                ps2 = ctx.ps_y[:, 512:1024]

                def gmm(e, g=g, ps=ps, wgsb=wgsb):
                    ins = None
                    for c in range(NC):
                        ins = e.matmul(ps, wgsb[:, c * 128:(c + 1) * 128],
                                       ctx.xnT[:, c * S + g * 512: c * S + (g + 1) * 512],
                                       start=(c == 0), stop=(c == NC - 1))
                    return ins
                P.emit("tensor", gmm, reads=[f"wgsb{jp}", f"xnT{g}"], writes=["ps_y0"])
                if L1F_SILU:
                    P.emit("scalar", lambda e, g=g, ps=ps: e.activation(
                        gsb[:, g * 512:(g + 1) * 512], ps, AF.Silu,
                        scale=ctx.scl[:, 1:2]),
                           reads=["ps_y0", "scl"], writes=["gsb"])
                else:
                    P.emit("scalar", lambda e, g=g, ps=ps: e.copy(
                        gsb[:, g * 512:(g + 1) * 512], ps),
                           reads=["ps_y0"], writes=["gsb"])

                def umm(e, g=g, ps2=ps2, wusb=wusb):
                    ins = None
                    for c in range(NC):
                        ins = e.matmul(ps2, wusb[:, c * 128:(c + 1) * 128],
                                       ctx.xnT[:, c * S + g * 512: c * S + (g + 1) * 512],
                                       start=(c == 0), stop=(c == NC - 1))
                    return ins
                P.emit("tensor", umm, reads=[f"wusb{jp}", f"xnT{g}"], writes=["ps_y1"])
                P.emit("vector", lambda e, j=j, g=g, ps2=ps2: e.tensor_tensor(
                    hfm[:, j * S + g * 512: j * S + (g + 1) * 512], ps2,
                    gsb[:, g * 512:(g + 1) * 512], ALU.mult),
                       reads=["ps_y1", "gsb"], writes=["hfm"])
        for tc in range(NT if L1F_DOWN else 0):
            for half in range(2):
                ps = ctx.ps_y[:, half * 512:(half + 1) * 512]

                def dmm(e, tc=tc, half=half, ps=ps):
                    ins = None
                    for j in range(L1F_JCN):
                        ins = e.matmul(ps, hfm[:, j * S + tc * 128: j * S + tc * 128 + 128],
                                       wdT[:, j * D + half * 512: j * D + (half + 1) * 512],
                                       start=(j == 0), stop=(j == L1F_JCN - 1))
                    return ins
                P.emit("tensor", dmm, reads=["hfm", "wdT"], writes=[f"ps_y{half}"])
                if L1F_EVAC == "stt":
                    P.emit("vector", lambda e, tc=tc, half=half, ps=ps:
                           e.scalar_tensor_tensor(
                               ctx.X[:, tc * D + half * 512: tc * D + (half + 1) * 512],
                               ps, ctx.scl[:, 2:3],
                               ctx.X[:, tc * D + half * 512: tc * D + (half + 1) * 512],
                               ALU.mult, ALU.add),
                           reads=[f"ps_y{half}", "scl", x_tile_region(tc)],
                           writes=[x_tile_region(tc)])
                elif L1F_EVAC == "add":
                    P.emit("vector", lambda e, tc=tc, half=half, ps=ps:
                           e.tensor_tensor(
                               ctx.X[:, tc * D + half * 512: tc * D + (half + 1) * 512],
                               ctx.X[:, tc * D + half * 512: tc * D + (half + 1) * 512],
                               ps, ALU.add),
                           reads=[f"ps_y{half}", x_tile_region(tc)],
                           writes=[x_tile_region(tc)])
    P.barrier()
    pool.pop()



def build_program(blocks, force_edges=True):
    nc = bass.Bass("TRN2", target_bir_lowering=False,
                   detect_race_conditions=False)
    ctx = Ctx()
    ctx.nc = nc
    P = ctx.prog = Prog(nc, force_edges=force_edges)

    ctx.d_x = nc.dram_tensor("x", [SEQ_PER_CORE, S, D], F32, kind="ExternalInput")
    ctx.d_out = nc.dram_tensor("out", [SEQ_PER_CORE, S, D], F32, kind="ExternalOutput")
    ctx.d_c = {}
    ctx.d_spill = nc.dram_tensor("spill", [128, NT * D], F32, kind="Internal")
    shapes = {
        "c_ident": ([128, 128], F32), "c_mask": ([128, 128], F32),
        "c_tabA": ([128, 2 * S], F32), "c_permA": ([128, 128], F32),
        "c_tabB": ([128, 2 * S], F32), "c_permB": ([128, 128], F32),
        "c_l0a_wq": ([8, 128, 1024], F32), "c_l0a_wk": ([8, 128, 1024], F32),
        "c_l0a_wv": ([4, 128, 2048], F32), "c_l0a_wo": ([128, 8 * D], F32),
        "c_l0a_bq": ([128, 8], F32), "c_l0a_bk": ([128, 8], F32),
        "c_l0a_bv": ([1, D], F32), "c_l0a_bo": ([1, D], F32),
        "c_l0f_wq": ([8, 128, 1024], F32), "c_l0f_wkva": ([128, 8 * 256], F32),
        "c_l0f_wkvb": ([128, 512], F32), "c_l0f_wo": ([128, 4 * D], F32),
        "c_l0f_wg8": ([8, 128, 8 * 512], F32), "c_l0f_wu8": ([8, 128, 8 * 512], F32),
        "c_l0f_wd8": ([8, 128, 4 * D], F32),
        "c_l0f_swg": ([128, 8 * 512], F32), "c_l0f_swu": ([128, 8 * 512], F32),
        "c_l0f_swd": ([128, 4 * D], F32),
        "c_l0f_wgate": ([128, 8 * 8], F32), "c_l0f_ebias": ([1, 8], F32),
        "c_l0f_wln1": ([1, D], BF16), "c_l0f_wln2": ([1, D], BF16),
        "c_l1a_wdq": ([128, 8 * 128], F32), "c_l1a_wdkv": ([128, 8 * 128], F32),
        "c_l1a_wuq": ([128, 8 * 128], F32), "c_l1a_wuk": ([128, 8 * 128], F32),
        "c_l1a_wuv": ([128, 8 * 128], F32), "c_l1a_wot": ([128, 8 * D], F32),
        "c_l1a_wdqraw": ([128, D], F32),
        "c_l1f_wg": ([128, 24 * 1024], I32), "c_l1f_wu": ([128, 24 * 1024], I32),
        "c_l1f_wd": ([24, 128, 1024], I32),
    }
    need = {"c_ident", "c_mask"}
    if "l0_attn" in blocks:
        need |= {"c_tabA", "c_permA",
                 "c_l0a_wq", "c_l0a_wk", "c_l0a_wv", "c_l0a_wo",
                 "c_l0a_bq", "c_l0a_bk", "c_l0a_bv", "c_l0a_bo"}
    if "l0_ffn" in blocks:
        need |= {"c_tabB", "c_permB", "c_l0f_wq", "c_l0f_wkva", "c_l0f_wkvb",
                 "c_l0f_wo", "c_l0f_wg8", "c_l0f_wu8", "c_l0f_wd8",
                 "c_l0f_swg", "c_l0f_swu", "c_l0f_swd", "c_l0f_wgate",
                 "c_l0f_ebias", "c_l0f_wln1", "c_l0f_wln2"}
    if "l1_attn" in blocks:
        need |= {"c_l1a_wdq", "c_l1a_wdkv", "c_l1a_wuq", "c_l1a_wuk",
                 "c_l1a_wuv", "c_l1a_wot", "c_l1a_wdqraw"}
    if "l1_ffn" in blocks:
        need |= {"c_l1f_wg", "c_l1f_wu", "c_l1f_wd"}
    for nm in sorted(need):
        sh, dt = shapes[nm]
        ctx.d_c[nm] = nc.dram_tensor(nm, sh, dt, kind="ExternalInput")

    import contextlib
    es = contextlib.ExitStack()
    pool = Pool(es.enter_context(nc.sbuf_tensor("pool", [128, 206 * 1024], U8)),
                206 * 1024)
    ctx.pool = pool
    ctx.X = pool.carve(128, NT * D, F32, "X")
    ctx.xnT = pool.carve(128, NC * S, BF16, "xnT")
    ctx.xnT_h3 = ctx.xnT.rearrange("p (c t) -> p c t", c=NC)
    ctx.fst = pool.carve(128, 128, F32, "fst")
    ctx.sbank_ctr = 0
    ctx.pt_par = 0
    ctx.pt_half = 0
    ctx.evac_ctr = 0
    ctx.scl = pool.carve(128, 4, F32, "scl")
    ctx.perm = pool.carve(128, 128, BF16, "perm")
    ctx.identity = pool.carve(128, 128, BF16, "ident")
    ctx.mask = pool.carve(128, 128, BF16, "mask")
    ctx.t1 = pool.carve(128, 512, F32, "t1")
    ctx.t2 = pool.carve(128, 512, BF16, "t2")
    t1b = pool.carve(128, 512, F32, "t1b")
    t2b = pool.carve(128, 512, BF16, "t2b")
    ctx.T1 = [ctx.t1, t1b]
    ctx.T2 = [ctx.t2, t2b]
    ctx.sq = pool.carve(128, D, F32, "sq")
    ctx.xn = pool.carve(128, D, BF16, "xn")
    ctx.stats = pool.carve(128, 8, F32, "stats")
    ctx.stats2 = pool.carve(128, 8, F32, "stats2")

    ctx.ps_s = es.enter_context(nc.psum_tensor("ps_s", [128, 2048], F32))
    ctx.ps_pt = es.enter_context(nc.psum_tensor("ps_pt", [128, 512], F32))
    ctx.ps_pt_bf = ctx.ps_pt[:, :].bitcast(BF16)
    ctx.ps_ctx = es.enter_context(nc.psum_tensor("ps_ctx", [128, 512], F32))
    ctx.ps_ctx_bf = ctx.ps_ctx[:, :].bitcast(BF16)
    ctx.ps_y = es.enter_context(nc.psum_tensor("ps_y", [128, 1024], F32))
    ctx.ps_y_bf = ctx.ps_y[:, :].bitcast(BF16)

    P.emit("vector", lambda e: e.memset(ctx.scl[:, 0:1], 0.125), writes=["scl"])
    P.emit("vector", lambda e: e.memset(ctx.scl[:, 1:2], 1.0 / 127.0), writes=["scl"])
    P.emit("vector", lambda e: e.memset(ctx.scl[:, 2:3], 1.0 / 16129.0), writes=["scl"])
    if "c_ident" in ctx.d_c:
        P.dma("gpsimd", lambda e: e.dma_start(ctx.identity[:, :], ctx.d_c["c_ident"][:, :]),
              writes=["ident"])
        P.dma("gpsimd", lambda e: e.dma_start(ctx.mask[:, :], ctx.d_c["c_mask"][:, :]),
              writes=["mask"])

    ctx.dbg = {}
    for seq in range(SEQ_PER_CORE):
        emit_load_x(ctx, seq)
        for b in blocks:
            if b == "l0_attn":
                emit_l0_attn(ctx, seq)
            elif b == "l0_ffn":
                emit_l0_ffn(ctx, seq)
            elif b == "l1_attn":
                emit_l1_attn(ctx, seq)
            elif b == "l1_ffn":
                emit_l1_ffn(ctx, seq)
            else:
                raise ValueError(b)
        if seq == 0:
            if DEBUG_DUMP:
                P.barrier()
            for nm in DEBUG_DUMP:
                ap = getattr(ctx, nm, None)
                if ap is None:
                    continue
                regs = [r for r in P.regions if r == nm or
                        (r.startswith(nm) and r[len(nm):].isdigit())]
                p, n = ap.shape
                d = nc.dram_tensor(f"dbg_{nm}", [p, n], ap.dtype, kind="ExternalOutput")
                ctx.dbg[nm] = d
                P.dma("gpsimd", lambda e, d=d, ap=ap: e.dma_start(d[:, :], ap),
                      reads=regs, writes=[f"dbgw_{nm}"])
            if DEBUG_DUMP:
                P.barrier()
        emit_store_x(ctx, seq)

    P.finish()
    P.build_block()
    es.close()
    nc._dbg_ctx = ctx
    return nc, sorted(ctx.d_c.keys())



def kernel(**inputs):
    consts = make_consts(inputs)
    x = np.ascontiguousarray(inputs["L0_attn_x"].astype(np.float32))
    shards = [x[i * SEQ_PER_CORE:(i + 1) * SEQ_PER_CORE] for i in range(N_CORE)]

    cur = shards
    for blocks in LAUNCH_PLAN:
        nc, cnames = build_program(blocks)
        in_maps = []
        for i in range(N_CORE):
            m = {"x": cur[i]}
            for nm in cnames:
                m[nm] = consts[nm]
            in_maps.append(m)
        res = run_bass_kernel_spmd(nc, in_maps, core_ids=list(range(N_CORE)))
        cur = [res.results[i]["out"] for i in range(N_CORE)]
    return np.concatenate(cur, axis=0).reshape(16, S, D)
